# version 06
# baseline (speedup 1.0000x reference)
import numpy as np
import ml_dtypes
from contextlib import ExitStack

import concourse.bass as bass
import concourse.mybir as mybir
from concourse.bass_utils import run_bass_kernel_spmd

F32 = mybir.dt.float32
BF16 = mybir.dt.bfloat16
AX = mybir.AxisListType.X
AF = mybir.ActivationFunctionType
ALU = mybir.AluOpType

P = 128
T_CORE = 1024
NT = 8
D = 2048
ND = 16
H, HKV, HD = 16, 4, 128
SEQ = 8192
NKT = 64
KILL = 30000.0
MAGIC = 1.5 * 2.0 ** 23
LN_EPS = 1e-5
RMS0_EPS = 1e-5
RMS1_EPS = float(np.finfo(np.float32).eps)
HID0 = 8192
HID1 = 5632


class Plan:
    ENGS = ("tensor", "scalar", "vector", "sync", "gpsimd")

    def __init__(self, nc):
        self.nc = nc
        self.ops = {e: [] for e in self.ENGS}
        self.sems = {}
        self.cnt = {}
        self.stack = ExitStack()

    def sem(self, name):
        if name not in self.sems:
            self.sems[name] = self.stack.enter_context(self.nc.semaphore(name))
            self.cnt[name] = 0
        return self.sems[name]

    def sbuf(self, name, shape, dtype):
        return self.stack.enter_context(self.nc.sbuf_tensor(name, shape, dtype))

    def psum(self, name, shape, dtype):
        return self.stack.enter_context(self.nc.psum_tensor(name, shape, dtype))

    def emit(self, eng, fn):
        self.ops[eng].append(fn)

    def wait(self, eng, sem_name, val=None):
        s = self.sem(sem_name)
        v = self.cnt[sem_name] if val is None else val
        if v > 0:
            self.ops[eng].append(lambda e: e.wait_ge(s, v))

    def inc(self, eng, fn, sem_name, by):
        s = self.sem(sem_name)
        self.cnt[sem_name] += by
        self.ops[eng].append(lambda e: fn(e).then_inc(s, by))
        return self.cnt[sem_name]

    def dma(self, eng, out, in_, sem_name, **kw):
        return self.inc(eng, lambda e: e.dma_start(out=out, in_=in_, **kw),
                        sem_name, 16)

    def build(self):
        with self.nc.Block() as block:
            @block.tensor
            def _(e):
                for f in self.ops["tensor"]:
                    f(e)

            @block.scalar
            def _(e):
                for f in self.ops["scalar"]:
                    f(e)

            @block.vector
            def _(e):
                for f in self.ops["vector"]:
                    f(e)

            @block.sync
            def _(e):
                for f in self.ops["sync"]:
                    f(e)

            @block.gpsimd
            def _(e):
                for f in self.ops["gpsimd"]:
                    f(e)


def build_program(n_sub=4, dbg=False, barriers=0, dbgmask=0xffff):
    nc = bass.Bass("TRN2", target_bir_lowering=False)
    pl = Plan(nc)
    dt = {}

    def dram(name, shape, dtype=F32, kind=None, shared=False):
        kw = {}
        if kind:
            kw["kind"] = kind
        if shared:
            kw["addr_space"] = "Shared"
        t = nc.dram_tensor(name, shape, dtype, **kw)
        dt[name] = t
        return t

    dram("x8", [NT, P, D], kind="ExternalInput")
    dram("cos8", [NT, P, HD // 2], kind="ExternalInput")
    dram("sin8", [NT, P, HD // 2], kind="ExternalInput")
    dram("Wq", [ND, P, D], kind="ExternalInput")
    dram("Wk", [ND, P, HKV * HD], kind="ExternalInput")
    dram("Wv", [ND, P, HKV * HD], kind="ExternalInput")
    dram("Wo", [ND, P, D], kind="ExternalInput")
    dram("bias4", [4, D], kind="ExternalInput")
    dram("shift", [P, NKT], kind="ExternalInput")
    dram("dmask0", [P, T_CORE], kind="ExternalInput")
    dram("ident", [P, P], kind="ExternalInput")
    if n_sub >= 2:
        dram("w_norm", [1, D], kind="ExternalInput")
        dram("gv_sl", [2, P, 2 * HID0], kind="ExternalInput")
        dram("wo_sl", [NT, P, D], kind="ExternalInput")
    if n_sub >= 3:
        for w in ("wqT1", "wkT1", "wvT1", "wgT1", "wpT1"):
            dram(w, [ND, P, D], kind="ExternalInput")
        dram("pmask", [P, 8], kind="ExternalInput")
    if n_sub >= 4:
        dram("w1_sl", [2, P, HID1], kind="ExternalInput")
        dram("w2_sl", [2, P, HID1], kind="ExternalInput")
        dram("w3_sl", [44, P, 256], kind="ExternalInput")
    dram("out8", [NT, P, D], kind="ExternalOutput")
    if dbg:
        dram("dbg", [48, P, D], kind="ExternalOutput")
    dbg_row = [0]

    dram("kvT_in", [8, P, T_CORE], BF16)
    dram("kvT_g", [64, P, T_CORE], BF16, shared=True)
    if n_sub >= 2:
        dram("gm_in", [1, 8])
        dram("gm_out", [1, 8], shared=True)
        dram("gv_q", [2, P, 2 * HID0], BF16)
        dram("gv_qg", [16, P, 2 * HID0], BF16, shared=True)
        dram("wo_q", [NT, P, D], BF16)
        dram("wo_qg", [64, P, D], BF16, shared=True)
        dram("u_scr", [NT, P, HID0], BF16)
    if n_sub >= 3:
        for s in range(4):
            dram(f"ct_in{s}", [ND, P, 512], BF16)
            dram(f"ct_g{s}", [8 * ND, P, 512], BF16, shared=True)
        dram("oo_scr", [ND, P, T_CORE], BF16)
        dram("vt_scr", [ND, P, T_CORE], BF16)
        for w in ("wqT1", "wkT1", "wvT1", "wgT1", "wpT1"):
            dram(w + "b", [ND, P, D], BF16)
        dram("gt_scr", [ND, P, T_CORE], BF16)
        dram("x_scr", [NT, P, D])
    if n_sub >= 4:
        dram("w1_q", [2, P, HID1], BF16)
        dram("w1_qg", [16, P, HID1], BF16, shared=True)
        dram("w2_q", [2, P, HID1], BF16)
        dram("w2_qg", [16, P, HID1], BF16, shared=True)
        dram("w3_q", [44, P, 256], BF16)
        dram("w3_qg", [352, P, 256], BF16, shared=True)
        dram("hid_scr", [NT, P, HID1], BF16)

    xa = pl.sbuf("xa", [P, 16384], F32)
    BA_COLS = 61440
    ba = pl.sbuf("ba", [P, BA_COLS], BF16)
    RQ, RA, RB, RW, RC, RE, RI, RM, RS = (0, 16384, 32768, 40960, 49152, 57344,
                                          59648, 59776, 60800)
    RWB = 60928
    stage = pl.sbuf("stage", [P, D], F32)
    den = pl.sbuf("den", [P, T_CORE], F32)
    cosb = pl.sbuf("cosb", [P, NT * 64], F32)
    sinb = pl.sbuf("sinb", [P, NT * 64], F32)
    shiftb = pl.sbuf("shiftb", [P, NKT], F32)
    small = pl.sbuf("small", [P, 64], F32)
    red = pl.sbuf("red", [P, 64], F32)
    ones_c = pl.sbuf("ones_c", [P, 1], BF16)

    ps_a = pl.psum("ps_a", [P, T_CORE], F32)
    ps_b = pl.psum("ps_b", [P, T_CORE], F32)
    ps_c = pl.psum("ps_c", [P, T_CORE], F32)
    ps_d = pl.psum("ps_d", [P, 512], F32)
    ps_t = pl.psum("ps_t", [P, T_CORE], BF16)

    def B(off, w):
        return ba[:, off:off + w]

    tr_state = {"pend": 0}

    def transpose_batch(pairs, extra_wait=None):
        for g0 in range(0, len(pairs), 8):
            grp = pairs[g0:g0 + 8]
            pl.wait("tensor", "trv", pl.cnt.get("trv", 0))
            for i, (ia, oa) in enumerate(grp):
                def f(e, ia=ia, i=i):
                    return e.transpose(ps_t[:, i * P:(i + 1) * P], ia, B(RI, P))
                if i == len(grp) - 1:
                    pl.inc("tensor", f, "tr", 1)
                else:
                    pl.emit("tensor", f)
            pl.wait("vector", "tr", pl.cnt["tr"])
            for i, (ia, oa) in enumerate(grp):
                def fv(e, oa=oa, i=i):
                    return e.tensor_copy(oa, ps_t[:, i * P:(i + 1) * P])
                if i == len(grp) - 1:
                    pl.inc("vector", fv, "trv", 1)
                else:
                    pl.emit("vector", fv)

    def tree_bcast(buf, width):
        n = 1
        while n < P:
            m = min(n, P - n)
            def f(e, n=n, m=m):
                return e.tensor_copy(buf[n:n + m, 0:width], buf[0:m, 0:width])
            pl.emit("vector", f)
            pl.emit("vector", lambda e: e.drain())
            n += m

    def tree_reduce_sum(buf, width, out_ap):
        pl.emit("vector", lambda e: e.tensor_copy(red[0:64, 0:width], buf[64:128, 0:width]))
        pl.emit("vector", lambda e: e.drain())
        pl.emit("vector", lambda e: e.tensor_tensor(red[0:64, 0:width], red[0:64, 0:width], buf[0:64, 0:width], ALU.add))
        n = 32
        while n >= 1:
            pl.emit("vector", lambda e: e.drain())
            def f(e, n=n):
                return e.tensor_copy(buf[0:n, 0:width], red[n:2 * n, 0:width])
            pl.emit("vector", f)
            pl.emit("vector", lambda e: e.drain())
            def f2(e, n=n):
                return e.tensor_tensor(red[0:n, 0:width], red[0:n, 0:width], buf[0:n, 0:width], ALU.add)
            pl.emit("vector", f2)
            n //= 2
        pl.emit("vector", lambda e: e.drain())
        pl.emit("vector", lambda e: e.tensor_copy(out_ap, red[0:1, 0:width]))
        pl.emit("vector", lambda e: e.drain())

    def norm_stats(src_ap, width, eps, kind, rstd_col=0):
        pl.wait("scalar", "vv", pl.cnt["vv"])
        pl.inc("scalar", lambda e: e.activation(B(RC + 4096, width), src_ap, AF.Square,
                                                accum_out=small[:, 8:9]), "ex", 1)
        if kind == "ln":
            pl.emit("vector", lambda e: e.reduce_sum(small[:, 9:10], src_ap, axis=AX))
            pl.emit("vector", lambda e: e.drain())
            pl.emit("vector", lambda e: e.tensor_scalar(small[:, 10:11], small[:, 9:10], 1.0 / width, None, ALU.mult))
            pl.emit("vector", lambda e: e.drain())
        pl.wait("vector", "ex", pl.cnt["ex"])
        if kind == "ln":
            pl.emit("vector", lambda e: e.tensor_tensor(small[:, 11:12], small[:, 10:11], small[:, 10:11], ALU.mult))
            pl.emit("vector", lambda e: e.drain())
            pl.emit("vector", lambda e: e.scalar_tensor_tensor(small[:, 12:13], small[:, 8:9], 1.0 / width, small[:, 11:12], ALU.mult, ALU.subtract))
        else:
            pl.emit("vector", lambda e: e.tensor_scalar(small[:, 12:13], small[:, 8:9], 1.0 / width, None, ALU.mult))
        pl.emit("vector", lambda e: e.drain())
        pl.inc("vector", lambda e, eps=eps: e.tensor_scalar(small[:, 13:14], small[:, 12:13], eps, None, ALU.add), "vv", 1)
        pl.wait("scalar", "vv", pl.cnt["vv"])
        pl.inc("scalar", lambda e: e.activation(small[:, 14:15], small[:, 13:14], AF.Sqrt), "ex", 1)
        pl.wait("vector", "ex", pl.cnt["ex"])
        pl.emit("vector", lambda e: e.reciprocal(small[:, rstd_col:rstd_col + 1], small[:, 14:15]))
        pl.emit("vector", lambda e: e.drain())
        if kind == "ln":
            pl.emit("vector", lambda e: e.tensor_tensor(small[:, 15:16], small[:, 10:11], small[:, rstd_col:rstd_col + 1], ALU.mult))
            pl.emit("vector", lambda e: e.drain())
            pl.inc("vector", lambda e: e.tensor_scalar(small[:, 1:2], small[:, 15:16], -1.0, None, ALU.mult), "vv", 1)
        else:
            pl.inc("vector", lambda e: e.engine_nop(), "vv", 1)
        pl.wait("scalar", "vv", pl.cnt["vv"])

    def sandwich_tail(yt_ap, t, eps=LN_EPS):
        norm_stats(yt_ap, D, eps, "ln")
        pl.inc("scalar", lambda e: e.activation(stage[:, 0:D], yt_ap, AF.Identity,
                                                scale=small[:, 0:1], bias=small[:, 1:2]), "ex", 1)
        pl.wait("vector", "ex", pl.cnt["ex"])
        xt = xa[:, t * D:(t + 1) * D]
        pl.inc("vector", lambda e, xt=xt: e.tensor_tensor(xt, xt, stage[:, 0:D], ALU.add), "vv", 1)

    panel_guard = {}

    def load_panel(src_name, c0, w, dst_off, nchunk=ND, sem="wl", key=None):
        key = key or dst_off
        if panel_guard.get(key):
            pl.wait("gpsimd", "mm", panel_guard[key])
        for c in range(nchunk):
            pl.dma("gpsimd", B(dst_off + c * w, w), dt[src_name].ap()[c][:, c0:c0 + w], sem)
        return pl.cnt[sem]

    def phase_barrier(bit):
        if not (barriers & (1 << bit)):
            return
        for s in list(pl.cnt.keys()):
            for eng in ("tensor", "scalar", "vector", "sync", "gpsimd"):
                pl.wait(eng, s, pl.cnt.get(s, 0))

    def dbg_dump(ap):
        if not dbg:
            return
        r = dbg_row[0]
        dbg_row[0] += 1
        pl.wait("sync", "vv", pl.cnt["vv"])
        pl.wait("sync", "ex", pl.cnt["ex"])
        w = ap.shape[-1] if hasattr(ap, "shape") else D
        pl.dma("sync", dt["dbg"].ap()[r][:, 0:w], ap, "st")

    def bias_mm(pst, brow_slice, start=False, stop=False):
        def f(e):
            return e.matmul(pst, ba[0:1, RS:RS + P], brow_slice, start=start, stop=stop)
        return f

    for t in range(NT):
        pl.dma("sync", xa[:, t * D:(t + 1) * D], dt["x8"].ap()[t], "ld")
    pl.dma("gpsimd", B(RI, P), dt["ident"].ap(), "ld")
    for t in range(NT):
        pl.dma("sync", cosb[:, t * 64:(t + 1) * 64], dt["cos8"].ap()[t], "ld")
        pl.dma("sync", sinb[:, t * 64:(t + 1) * 64], dt["sin8"].ap()[t], "ld")
    pl.dma("sync", shiftb[:], dt["shift"].ap(), "ld")
    pl.dma("gpsimd", B(RM, T_CORE), dt["dmask0"].ap(), "ld")
    ld_all = pl.cnt["ld"]
    pl.emit("vector", lambda e: e.memset(ones_c[:], 1.0))
    pl.emit("vector", lambda e: e.memset(ba[0:1, RS:RS + P], 1.0))
    pl.emit("vector", lambda e: e.memset(small[:, 20:21], 0.0))
    pl.inc("vector", lambda e: e.engine_nop(), "vv", 1)
    pl.wait("vector", "ld", ld_all)
    pl.wait("scalar", "ld", ld_all)
    pl.wait("tensor", "ld", ld_all)

    for t in range(NT):
        xt = xa[:, t * D:(t + 1) * D]
        norm_stats(xt, D, LN_EPS, "ln")
        pl.inc("scalar", lambda e, xt=xt: e.activation(B(RE, D), xt, AF.Identity,
                                                       scale=small[:, 0:1], bias=small[:, 1:2]), "ex", 1)
        pl.wait("tensor", "ex", pl.cnt["ex"])
        pairs = [(B(RE + dc * P, P), B(RA + (t * ND + dc) * P, P)) for dc in range(ND)]
        transpose_batch(pairs)
    xlT_ready = pl.cnt["trv"]

    def rope_tile(src_f32_off, dst_bf_ap, t):
        cs = cosb[:, t * 64:(t + 1) * 64]
        sn = sinb[:, t * 64:(t + 1) * 64]
        a = stage[:, src_f32_off + 0:src_f32_off + P:2]
        b = stage[:, src_f32_off + 1:src_f32_off + P:2]
        oe = dst_bf_ap[:, 0:P:2]
        oo = dst_bf_ap[:, 1:P:2]
        ta = den[:, 0:64]
        tb = den[:, 64:128]
        pl.emit("vector", lambda e: e.tensor_tensor(ta, a, cs, ALU.mult))
        pl.emit("vector", lambda e: e.tensor_tensor(tb, b, sn, ALU.mult))
        pl.emit("vector", lambda e: e.drain())
        pl.emit("vector", lambda e: e.tensor_tensor(oe, ta, tb, ALU.subtract))
        pl.emit("vector", lambda e: e.tensor_tensor(ta, a, sn, ALU.mult))
        pl.emit("vector", lambda e: e.tensor_tensor(tb, b, cs, ALU.mult))
        pl.emit("vector", lambda e: e.drain())
        pl.emit("vector", lambda e: e.tensor_tensor(oo, ta, tb, ALU.add))
        pl.emit("vector", lambda e: e.drain())

    pl.wait("tensor", "trv", xlT_ready)
    jobs = [("Wq", D, "q"), ("Wk", HKV * HD, "k"), ("Wv", HKV * HD, "v")]
    for wname, wcols, which in jobs:
        for og in range(wcols // 512):
            wl_c = load_panel(wname, og * 512, 512, RW, key="RW")
            brow_r = {"q": 0, "k": 1, "v": 2}[which]
            wl_c = pl.dma("gpsimd", ba[0:1, RWB:RWB + 512],
                          dt["bias4"].ap()[brow_r:brow_r + 1, og * 512:(og + 1) * 512], "wl")
            pl.wait("tensor", "wl", wl_c)
            for t in range(NT):
                pst = ps_c[:, 0:512] if (t % 2) else ps_d[:, 0:512]
                for dc in range(ND):
                    lhsT = B(RA + (t * ND + dc) * P, P)
                    rhs = B(RW + dc * 512, 512)
                    def f(e, pst=pst, lhsT=lhsT, rhs=rhs, dc=dc):
                        return e.matmul(pst, lhsT, rhs, start=(dc == 0), stop=False)
                    pl.emit("tensor", f)
                pl.inc("tensor", bias_mm(pst, ba[0:1, RWB:RWB + 512], stop=True), "mm", 1)
                pl.wait("vector", "mm", pl.cnt["mm"])
                pl.inc("vector", lambda e, pst=pst: e.tensor_copy(stage[:, 0:512], pst), "vv", 1)
                pl.wait("tensor", "vv", pl.cnt["vv"])
                if which in ("q", "k"):
                    for hs in range(4):
                        h = og * 4 + hs
                        dstbf = B(RE + P, P)
                        rope_tile(hs * P, dstbf, t)
                        pl.inc("vector", lambda e: e.engine_nop(), "vv", 1)
                        pl.wait("tensor", "vv", pl.cnt["vv"])
                        if which == "q":
                            oa = B(RQ + h * T_CORE + t * P, P)
                            transpose_batch([(dstbf, oa)])
                        else:
                            oa = B(RC + hs * T_CORE + t * P, P)
                            transpose_batch([(dstbf, oa)])
                            pl.wait("vector", "trv", pl.cnt["trv"])
                            pl.inc("vector", lambda e, oa=oa, hs=hs, t=t:
                                   e.tensor_copy(B(RB + hs * T_CORE + t * P, P), oa), "vv", 1)
                else:
                    pl.inc("vector", lambda e, t=t:
                           e.tensor_copy(B(RC + 4096 + t * 512, 512), stage[:, 0:512]), "vv", 1)
                    pl.wait("tensor", "vv", pl.cnt["vv"])
                    pairs = []
                    for hs in range(4):
                        ia = B(RC + 4096 + t * 512 + hs * P, P)
                        oa = B(RB + (4 + hs) * T_CORE + t * P, P)
                        pairs.append((ia, oa))
                    transpose_batch(pairs)
            panel_guard["RW"] = pl.cnt["mm"]

    pl.wait("gpsimd", "vv", pl.cnt["vv"])
    pl.wait("gpsimd", "trv", pl.cnt["trv"])
    for b8 in range(8):
        pl.dma("gpsimd", dt["kvT_in"].ap()[b8], B(RB + b8 * T_CORE, T_CORE), "gd")
    pl.wait("gpsimd", "gd", pl.cnt["gd"])
    pl.inc("gpsimd", lambda e: e.collective_compute(
        "AllGather", ALU.bypass, replica_groups=[list(range(8))],
        ins=[dt["kvT_in"].ap().opt()], outs=[dt["kvT_g"].ap().opt()]), "cc", 1)
    cc_kv = pl.cnt["cc"]


    pl.wait("sync", "cc", cc_kv)
    for j in range(HKV):
        if j > 0:
            pl.wait("sync", "mm", pl.cnt["mm"])
        for c8 in range(8):
            pl.dma("sync", B(RB + c8 * T_CORE, T_CORE), dt["kvT_g"].ap()[c8 * 8 + j], "kl")
            pl.dma("sync", B(RW + c8 * T_CORE, T_CORE), dt["kvT_g"].ap()[c8 * 8 + 4 + j], "kl")
        pl.wait("tensor", "kl", pl.cnt["kl"])
        pairs = [(B(RW + kt * P, P), B(RW + kt * P, P)) for kt in range(NKT)]
        transpose_batch(pairs)
        pl.wait("tensor", "trv", pl.cnt["trv"])
        for h in (j, j + 4, j + 8, j + 12):
            qrhs = B(RQ + h * T_CORE, T_CORE)
            niter = NKT + 8
            ready_ex = [0] * niter
            ready_vv = [0] * niter
            for step in range(niter + 1):
                if step < niter:
                    it = step
                    diag = it >= NKT
                    kt = it if not diag else it - NKT
                    ps_s = ps_a if (it % 2 == 0) else ps_b
                    wslot = B(RE + (it % 2) * T_CORE, T_CORE)
                    if it >= 2:
                        pl.wait("tensor", "ex", ready_ex[it - 2])
                        if it - 2 >= NKT:
                            pl.wait("tensor", "vv", ready_vv[it - 2])
                    if diag:
                        lhsT = B(RC + j * T_CORE + kt * P, P)
                    else:
                        lhsT = B(RB + kt * P, P)
                    for half in range(2):
                        def f(e, ps_s=ps_s, lhsT=lhsT, qr=qrhs, half=half):
                            return e.matmul(ps_s[:, half * 512:(half + 1) * 512], lhsT,
                                            qr[:, half * 512:(half + 1) * 512],
                                            start=True, stop=True)
                        if half == 1:
                            pl.inc("tensor", f, "mm", 1)
                        else:
                            pl.emit("tensor", f)
                    pl.wait("scalar", "mm", pl.cnt["mm"])
                    if diag:
                        def fe(e, ps_s=ps_s, w=wslot):
                            return e.activation(w, ps_s[:], AF.Exp, scale=0.25,
                                                bias=small[:, 20:21])
                    else:
                        def fe(e, ps_s=ps_s, w=wslot, kt=kt):
                            return e.activation(w, ps_s[:], AF.Exp, scale=0.25,
                                                bias=shiftb[:, kt:kt + 1])
                    ready_ex[it] = pl.inc("scalar", fe, "ex", 1)
                    if diag:
                        pl.wait("vector", "ex", ready_ex[it])
                        d8 = kt
                        def fm(e, w=wslot, d8=d8):
                            if d8 == 0:
                                return e.tensor_tensor(w, w, B(RM, T_CORE), ALU.mult)
                            return e.tensor_tensor(w[:, d8 * P:T_CORE], w[:, d8 * P:T_CORE],
                                                   B(RM, T_CORE - d8 * P), ALU.mult)
                        pl.emit("vector", fm)
                        if d8 > 0:
                            pl.emit("vector", lambda e, w=wslot, d8=d8: e.memset(w[:, 0:d8 * P], 0.0))
                        ready_vv[it] = pl.inc("vector", lambda e: e.engine_nop(), "vv", 1)
                if step >= 1:
                    it = step - 1
                    diag = it >= NKT
                    kt = it if not diag else it - NKT
                    wslot = B(RE + (it % 2) * T_CORE, T_CORE)
                    if diag:
                        pl.wait("tensor", "vv", ready_vv[it])
                        vtile = B(RC + 4096 + kt * 512 + j * P, P)
                    else:
                        pl.wait("tensor", "ex", ready_ex[it])
                        vtile = B(RW + kt * P, P)
                    first = (it == 0)
                    last = (it == niter - 1)
                    for half in range(2):
                        def fp(e, vtile=vtile, w=wslot, half=half, first=first, last=last):
                            return e.matmul(ps_c[:, half * 512:(half + 1) * 512], vtile,
                                            w[:, half * 512:(half + 1) * 512],
                                            start=first, stop=last)
                        pl.emit("tensor", fp)
                    for half in range(2):
                        dn = ps_d[0:1, 0:512] if half == 0 else ps_d[32:33, 0:512]
                        def fd(e, dn=dn, w=wslot, half=half, first=first, last=last):
                            return e.matmul(dn, ones_c[:, 0:1],
                                            w[:, half * 512:(half + 1) * 512],
                                            start=first, stop=last)
                        if half == 1:
                            pl.inc("tensor", fd, "mm", 1)
                        else:
                            pl.emit("tensor", fd)
            pl.wait("vector", "mm", pl.cnt["mm"])
            pl.emit("vector", lambda e: e.tensor_copy(den[0:1, 0:512], ps_d[0:1, 0:512]))
            pl.emit("vector", lambda e: e.tensor_copy(den[0:1, 512:1024], ps_d[32:33, 0:512]))
            pl.emit("vector", lambda e: e.drain())
            pl.emit("vector", lambda e: e.reciprocal(den[0:1, 0:T_CORE], den[0:1, 0:T_CORE]))
            pl.emit("vector", lambda e: e.drain())
            pl.emit("vector", lambda e: e.tensor_copy(ba[0:1, RE:RE + T_CORE], den[0:1, 0:T_CORE]))
            pl.emit("vector", lambda e: e.drain())
            pl.emit("vector", lambda e: e.tensor_tensor(stage[0:1, 0:T_CORE], den[0:1, 0:T_CORE],
                                                        ba[0:1, RE:RE + T_CORE], ALU.subtract))
            pl.emit("vector", lambda e: e.drain())
            pl.inc("vector", lambda e: e.tensor_copy(ba[0:1, RE + T_CORE:RE + 2 * T_CORE],
                                                     stage[0:1, 0:T_CORE]), "vv", 1)
            pl.wait("tensor", "vv", pl.cnt["vv"])
            for half in range(2):
                def fb1(e, half=half):
                    return e.matmul(ps_a[:, half * 512:(half + 1) * 512], ba[0:1, RS:RS + P],
                                    ba[0:1, RE + half * 512:RE + half * 512 + 512],
                                    start=True, stop=False)
                pl.emit("tensor", fb1)
                def fb2(e, half=half):
                    return e.matmul(ps_a[:, half * 512:(half + 1) * 512], ba[0:1, RS:RS + P],
                                    ba[0:1, RE + T_CORE + half * 512:RE + T_CORE + half * 512 + 512],
                                    start=False, stop=True)
                if half == 1:
                    pl.inc("tensor", fb2, "mm", 1)
                else:
                    pl.emit("tensor", fb2)
            pl.wait("vector", "mm", pl.cnt["mm"])
            pl.emit("vector", lambda e: e.tensor_copy(den[:, 0:T_CORE], ps_a[:]))
            pl.emit("vector", lambda e: e.drain())
            pl.inc("vector", lambda e, h=h:
                   e.tensor_tensor(B(RA + h * T_CORE, T_CORE), ps_c[:], den[:, 0:T_CORE], ALU.mult),
                   "vv", 1)
            pl.wait("tensor", "vv", pl.cnt["vv"])

    panel_guard["RW"] = pl.cnt["mm"]
    for og in range(4):
        wl_c = load_panel("Wo", og * 512, 512, RW, key="RW")
        wl_c = pl.dma("gpsimd", ba[0:1, RWB:RWB + 512],
                      dt["bias4"].ap()[3:4, og * 512:(og + 1) * 512], "wl")
        pl.wait("tensor", "wl", wl_c)
        for t in range(NT):
            pst = ps_c[:, 0:512] if (t % 2) else ps_d[:, 0:512]
            for hc in range(ND):
                lhsT = B(RA + hc * T_CORE + t * P, P)
                rhs = B(RW + hc * 512, 512)
                def f(e, pst=pst, lhsT=lhsT, rhs=rhs, hc=hc):
                    return e.matmul(pst, lhsT, rhs, start=(hc == 0), stop=False)
                pl.emit("tensor", f)
            pl.inc("tensor", bias_mm(pst, ba[0:1, RWB:RWB + 512], stop=True), "mm", 1)
            pl.wait("vector", "mm", pl.cnt["mm"])
            dst = B(RQ + t * D + og * 512, 512)
            pl.inc("vector", lambda e, pst=pst, dst=dst: e.tensor_copy(dst, pst), "vv", 1)
            pl.wait("tensor", "vv", pl.cnt["vv"])
        panel_guard["RW"] = pl.cnt["mm"]
    for t in range(NT):
        sandwich_tail(B(RQ + t * D, D), t)

    if dbg >= 2:
        dbg_dump(xa[:, 0:D])
    def bcast_row(src_row_ap, w, dst_ap, psum_full):
        hi = ba[0:1, RW:RW + w]
        lo = ba[0:1, RW + w:RW + 2 * w]
        pl.emit("vector", lambda e: e.tensor_copy(hi, src_row_ap))
        pl.emit("vector", lambda e: e.drain())
        pl.emit("vector", lambda e: e.tensor_tensor(stage[0:1, 0:w], src_row_ap, hi, ALU.subtract))
        pl.emit("vector", lambda e: e.drain())
        pl.inc("vector", lambda e: e.tensor_copy(lo, stage[0:1, 0:w]), "vv", 1)
        pl.wait("tensor", "vv", pl.cnt["vv"])
        nch = (w + 511) // 512
        for c in range(nch):
            cw = min(512, w - c * 512)
            def f1(e, c=c, cw=cw):
                return e.matmul(psum_full[:, c * 512:c * 512 + cw], ba[0:1, RS:RS + P],
                                hi[:, c * 512:c * 512 + cw], start=True, stop=False)
            pl.emit("tensor", f1)
            def f2(e, c=c, cw=cw):
                return e.matmul(psum_full[:, c * 512:c * 512 + cw], ba[0:1, RS:RS + P],
                                lo[:, c * 512:c * 512 + cw], start=False, stop=True)
            if c == nch - 1:
                pl.inc("tensor", f2, "mm", 1)
            else:
                pl.emit("tensor", f2)
        pl.wait("vector", "mm", pl.cnt["mm"])
        pl.inc("vector", lambda e: e.tensor_copy(dst_ap, psum_full[:, 0:w]), "vv", 1)
        pl.wait("tensor", "vv", pl.cnt["vv"])

    def bcast_bf_pair(src_row_ap, w, dst_hi_off, dst_lo_off):
        pl.emit("vector", lambda e: e.tensor_copy(ba[0:1, RW:RW + w], src_row_ap))
        pl.emit("vector", lambda e: e.drain())
        pl.emit("vector", lambda e: e.tensor_tensor(stage[0:1, 0:w], src_row_ap,
                                                    ba[0:1, RW:RW + w], ALU.subtract))
        pl.emit("vector", lambda e: e.drain())
        pl.inc("vector", lambda e: e.tensor_copy(ba[0:1, RW + w:RW + 2 * w], stage[0:1, 0:w]), "vv", 1)
        pl.wait("tensor", "vv", pl.cnt["vv"])
        for pi, dstoff in ((0, dst_hi_off), (1, dst_lo_off)):
            nch = (w + 1023) // 1024
            for c in range(nch):
                cw = min(1024, w - c * 1024)
                for sc in range(0, cw, 512):
                    scw = min(512, cw - sc)
                    srcoff = RW + pi * w + c * 1024 + sc
                    def f1(e, srcoff=srcoff, sc=sc, scw=scw):
                        return e.matmul(ps_a[:, sc:sc + scw], ba[0:1, RS:RS + P],
                                        ba[0:1, srcoff:srcoff + scw],
                                        start=True, stop=True)
                    if sc + 512 >= cw:
                        pl.inc("tensor", f1, "mm", 1)
                    else:
                        pl.emit("tensor", f1)
                pl.wait("vector", "mm", pl.cnt["mm"])
                pl.inc("vector", lambda e, dstoff=dstoff, c=c, cw=cw:
                       e.tensor_copy(ba[:, dstoff + c * 1024:dstoff + c * 1024 + cw],
                                     ps_a[:, 0:cw]), "vv", 1)
                pl.wait("tensor", "vv", pl.cnt["vv"])

    if n_sub >= 2:
        pl.wait("sync", "ex", pl.cnt["ex"])
        pl.wait("sync", "vv", pl.cnt["vv"])
        pl.wait("vector", "vv", pl.cnt["vv"])
        pl.emit("vector", lambda e: e.memset(den[:, 0:8], 0.0))
        pl.emit("vector", lambda e: e.drain())
        prep = [("gv_sl", 2, 2 * HID0, 0), ("wo_sl", NT, D, 1)]
        if n_sub >= 4:
            prep += [("w1_sl", 2, HID1, 2), ("w2_sl", 2, HID1, 3), ("w3_sl", 44, 256, 4)]
        for src, nr, wdt, col in prep:
            for r in range(nr):
                for c0 in range(0, wdt, D):
                    cw = min(D, wdt - c0)
                    ld_c = pl.dma("sync", stage[:, 0:cw], dt[src].ap()[r][:, c0:c0 + cw], "pp")
                    pl.wait("vector", "pp", ld_c)
                    pl.emit("vector", lambda e, cw=cw: e.reduce_sum(small[:, 16:17], stage[:, 0:cw],
                                                                    axis=AX, apply_absolute_value=True))
                    pl.emit("vector", lambda e: e.drain())
                    def fa(e, col=col):
                        return e.tensor_tensor(den[:, col:col + 1], den[:, col:col + 1],
                                               small[:, 16:17], ALU.add)
                    pl.inc("vector", fa, "vv", 1)
                    pl.wait("sync", "vv", pl.cnt["vv"])
        pl.emit("vector", lambda e: e.tensor_copy(ba[:, RW:RW + 8], den[:, 0:8]))
        pl.emit("vector", lambda e: e.drain())
        pl.emit("vector", lambda e: e.tensor_tensor(red[:, 0:8], den[:, 0:8],
                                                    ba[:, RW:RW + 8], ALU.subtract))
        pl.emit("vector", lambda e: e.drain())
        pl.inc("vector", lambda e: e.tensor_copy(ba[:, RW + 256:RW + 264], red[:, 0:8]), "vv", 1)
        pl.wait("tensor", "vv", pl.cnt["vv"])
        pl.emit("tensor", lambda e: e.matmul(ps_d[0:1, 0:8], ones_c[:, 0:1],
                                             ba[:, RW:RW + 8], start=True, stop=False))
        pl.inc("tensor", lambda e: e.matmul(ps_d[0:1, 0:8], ones_c[:, 0:1],
                                            ba[:, RW + 256:RW + 264], start=False, stop=True),
               "mm", 1)
        pl.wait("vector", "mm", pl.cnt["mm"])
        pl.emit("vector", lambda e: e.memset(stage[0:1, 0:8], 0.0))
        pl.emit("vector", lambda e: e.drain())
        pl.emit("vector", lambda e: e.tensor_scalar(stage[0:1, 0:1], ps_d[0:1, 0:1],
                                                    1.0 / (2048.0 * 2 * HID0), None, ALU.mult))
        pl.emit("vector", lambda e: e.tensor_scalar(stage[0:1, 1:2], ps_d[0:1, 1:2],
                                                    1.0 / (2048.0 * HID0), None, ALU.mult))
        if n_sub >= 4:
            pl.emit("vector", lambda e: e.tensor_scalar(stage[0:1, 2:4], ps_d[0:1, 2:4],
                                                        1.0 / (2048.0 * HID1), None, ALU.mult))
            pl.emit("vector", lambda e: e.tensor_scalar(stage[0:1, 4:5], ps_d[0:1, 4:5],
                                                        1.0 / (2048.0 * HID1), None, ALU.mult))
        pl.inc("vector", lambda e: e.engine_nop(), "vv", 1)
        pl.wait("gpsimd", "vv", pl.cnt["vv"])
        pl.dma("gpsimd", dt["gm_in"].ap(), stage[0:1, 0:8], "gd")
        pl.wait("gpsimd", "gd", pl.cnt["gd"])
        pl.inc("gpsimd", lambda e: e.collective_compute(
            "AllReduce", ALU.add, replica_groups=[list(range(8))],
            ins=[dt["gm_in"].ap().opt()], outs=[dt["gm_out"].ap().opt()]), "cc", 1)
        pl.wait("sync", "cc", pl.cnt["cc"])
        pl.dma("sync", stage[0:1, 8:16], dt["gm_out"].ap(), "pp")
        pl.wait("vector", "pp", pl.cnt["pp"])
        pl.emit("vector", lambda e: e.tensor_scalar(stage[0:1, 16:18], stage[0:1, 8:10],
                                                    1e-5, None, ALU.max))
        pl.emit("vector", lambda e: e.drain())
        pl.emit("vector", lambda e: e.reciprocal(stage[0:1, 18:20], stage[0:1, 16:18]))
        pl.emit("vector", lambda e: e.drain())
        if n_sub >= 4:
            pl.emit("vector", lambda e: e.tensor_scalar(stage[0:1, 21:24], stage[0:1, 10:13],
                                                        1e-8, None, ALU.add))
            pl.emit("vector", lambda e: e.drain())
            pl.emit("vector", lambda e: e.reciprocal(stage[0:1, 20:23], stage[0:1, 21:24]))
            pl.emit("vector", lambda e: e.drain())
        pl.inc("vector", lambda e: e.engine_nop(), "vv", 1)
        bcast_row(stage[0:1, 16:24], 8, small[:, 48:56], ps_a[:, 0:8])
        if dbg >= 2:
            dbg_dump(small[:, 0:64])
        pl.wait("sync", "vv", pl.cnt["vv"])
        for src, nr, wdt, qdst in (("gv_sl", 2, 2 * HID0, "gv_q"),
                                   ("wo_sl", NT, D, "wo_q")):
            inv_col = {"gv_q": 50, "wo_q": 51}[qdst]
            for r in range(nr):
                for c0 in range(0, wdt, D):
                    ld_c = pl.dma("sync", stage[:, 0:D], dt[src].ap()[r][:, c0:c0 + D], "pp")
                    pl.wait("vector", "pp", ld_c)
                    pl.emit("vector", lambda e, inv_col=inv_col:
                            e.tensor_scalar(stage[:, 0:D], stage[:, 0:D],
                                            small[:, inv_col:inv_col + 1], MAGIC, ALU.mult, ALU.add))
                    pl.emit("vector", lambda e: e.drain())
                    pl.emit("vector", lambda e: e.tensor_scalar(B(RB, D), stage[:, 0:D],
                                                                MAGIC, None, ALU.subtract))
                    pl.emit("vector", lambda e: e.drain())
                    pl.inc("vector", lambda e: e.tensor_scalar(B(RB, D), B(RB, D), 1.0, -1.0,
                                                               ALU.min, ALU.max), "vv", 1)
                    pl.wait("sync", "vv", pl.cnt["vv"])
                    qd_c = pl.dma("sync", dt[qdst].ap()[r][:, c0:c0 + D], B(RB, D), "pq")
                    pl.wait("vector", "pq", qd_c)
        pl.wait("gpsimd", "pq", pl.cnt["pq"])
        pl.inc("gpsimd", lambda e: e.collective_compute(
            "AllGather", ALU.bypass, replica_groups=[list(range(8))],
            ins=[dt["gv_q"].ap().opt()], outs=[dt["gv_qg"].ap().opt()]), "cc", 1)
        pl.inc("gpsimd", lambda e: e.collective_compute(
            "AllGather", ALU.bypass, replica_groups=[list(range(8))],
            ins=[dt["wo_q"].ap().opt()], outs=[dt["wo_qg"].ap().opt()]), "cc", 1)
        cc_tern = pl.cnt["cc"]

        phase_barrier(0)
        pl.dma("sync", stage[0:1, 0:D], dt["w_norm"].ap(), "pp")
        pl.wait("vector", "pp", pl.cnt["pp"])
        bcast_bf_pair(stage[0:1, 0:D], D, RC, RC + D)

        if dbg >= 2:
            pl.wait("gpsimd", "vv", pl.cnt["vv"])
            pl.dma("gpsimd", dt["dbg"].ap()[7], B(RC, D), "st")
            pl.dma("gpsimd", dt["dbg"].ap()[8], B(RC + D, D), "st")
        phase_barrier(1)
        for t in range(NT):
            xt = xa[:, t * D:(t + 1) * D]
            norm_stats(xt, D, LN_EPS, "ln")
            pl.inc("scalar", lambda e, xt=xt, t=t: e.activation(B(RQ + t * D, D), xt, AF.Identity,
                                                                scale=small[:, 0:1], bias=small[:, 1:2]), "ex", 1)
            if dbg >= 2 and t == 0 and (dbgmask & 8):
                pl.wait("gpsimd", "ex", pl.cnt["ex"])
                pl.dma("gpsimd", dt["dbg"].ap()[9], B(RQ, D), "st")
                pl.dma("gpsimd", dt["dbg"].ap()[7][:, 0:64], small[:, 0:64], "st")
                pl.wait("scalar", "st", pl.cnt["st"])
                pl.wait("vector", "st", pl.cnt["st"])
            pl.wait("vector", "ex", pl.cnt["ex"])
            pl.inc("vector", lambda e: e.engine_nop(), "vv", 1)
            norm_stats(B(RQ + t * D, D), D, RMS0_EPS, "rms", rstd_col=2)
            for half in range(2):
                xl = B(RQ + t * D + half * 1024, 1024)
                pl.emit("vector", lambda e, xl=xl, half=half:
                        e.tensor_tensor(stage[:, half * 1024:(half + 1) * 1024], xl,
                                        B(RC + half * 1024, 1024), ALU.mult))
                pl.emit("vector", lambda e, xl=xl, half=half:
                        e.tensor_tensor(den[:, 0:1024], xl, B(RC + D + half * 1024, 1024), ALU.mult))
                pl.emit("vector", lambda e: e.drain())
                pl.emit("vector", lambda e, half=half:
                        e.tensor_tensor(stage[:, half * 1024:(half + 1) * 1024],
                                        stage[:, half * 1024:(half + 1) * 1024],
                                        den[:, 0:1024], ALU.add))
                pl.emit("vector", lambda e: e.drain())
            pl.emit("vector", lambda e: e.reduce_max(small[:, 3:4], stage[:, 0:D],
                                                     axis=AX, apply_absolute_value=True))
            pl.emit("vector", lambda e: e.drain())
            pl.emit("vector", lambda e: e.reciprocal(small[:, 4:5], small[:, 3:4]))
            pl.emit("vector", lambda e: e.drain())
            pl.emit("vector", lambda e: e.tensor_scalar(small[:, 5:6], small[:, 4:5], 127.0, None, ALU.mult))
            pl.emit("vector", lambda e: e.tensor_tensor(small[:, 6:7], small[:, 2:3], small[:, 3:4], ALU.mult))
            pl.emit("vector", lambda e: e.drain())
            pl.emit("vector", lambda e, t=t: e.scalar_tensor_tensor(small[:, 24 + t:25 + t],
                                                                    small[:, 6:7], 1.0 / 127.0,
                                                                    small[:, 48:49], ALU.mult, ALU.mult))
            pl.emit("vector", lambda e: e.drain())
            if dbg >= 2 and t == 0 and (dbgmask & 16):
                pl.inc("vector", lambda e: e.engine_nop(), "vv", 1)
                pl.wait("gpsimd", "vv", pl.cnt["vv"])
                pl.dma("gpsimd", dt["dbg"].ap()[15], stage[:, 0:D], "st")
                pl.dma("gpsimd", dt["dbg"].ap()[4], B(RQ, D), "st")
                pl.dma("gpsimd", dt["dbg"].ap()[5], B(RC, D), "st")
                pl.dma("gpsimd", dt["dbg"].ap()[6][:, 0:64], small[:, 0:64], "st")
                pl.wait("vector", "st", pl.cnt["st"])
            pl.emit("vector", lambda e: e.tensor_scalar(stage[:, 0:D], stage[:, 0:D],
                                                        small[:, 5:6], MAGIC, ALU.mult, ALU.add))
            pl.emit("vector", lambda e: e.drain())
            pl.inc("vector", lambda e: e.tensor_scalar(B(RE, D), stage[:, 0:D],
                                                       MAGIC, None, ALU.subtract), "vv", 1)
            pl.wait("tensor", "vv", pl.cnt["vv"])
            pairs = [(B(RE + dc * P, P), B(RA + (t * ND + dc) * P, P)) for dc in range(ND)]
            transpose_batch(pairs)
        if dbg >= 2 and (dbgmask & 64):
            dbg_dump(small[:, 0:64])
        hqT_done = pl.cnt["trv"]

        phase_barrier(2)
        pl.wait("sync", "cc", cc_tern)
        pl.wait("sync", "mm", pl.cnt["mm"])
        pl.wait("sync", "trv", hqT_done)
        pl.wait("tensor", "trv", hqT_done)
        pl.emit("vector", lambda e: e.memset(small[:, 32:40], 0.0))
        pl.inc("vector", lambda e: e.engine_nop(), "vv", 1)
        pair_guard = [0, 0]
        uq_guard = [0, 0]
        for og in range(16):
            wl_v = 0
            if panel_guard.get("RW2"):
                pl.wait("sync", "mm", panel_guard["RW2"])
            for dc in range(ND):
                pl.dma("sync", B(RW + dc * 512, 512), dt["gv_qg"].ap()[dc][:, og * 512:(og + 1) * 512], "wl")
                wl_v = pl.dma("sync", B(RB + dc * 512, 512),
                              dt["gv_qg"].ap()[dc][:, HID0 + og * 512:HID0 + og * 512 + 512], "wl")
            pl.wait("tensor", "wl", wl_v)
            for t in range(NT):
                pr = t % 2
                ps_v = ps_d[:, 0:512] if pr == 0 else ps_a[:, 0:512]
                ps_g = ps_c[:, 0:512] if pr == 0 else ps_b[:, 0:512]
                if pair_guard[pr]:
                    pl.wait("tensor", "vv", pair_guard[pr])
                for dc in range(ND):
                    lhsT = B(RA + (t * ND + dc) * P, P)
                    def fv(e, ps_v=ps_v, lhsT=lhsT, dc=dc):
                        return e.matmul(ps_v, lhsT, B(RW + dc * 512, 512),
                                        start=(dc == 0), stop=(dc == ND - 1))
                    pl.emit("tensor", fv)
                for dc in range(ND):
                    lhsT = B(RA + (t * ND + dc) * P, P)
                    def fg(e, ps_g=ps_g, lhsT=lhsT, dc=dc):
                        return e.matmul(ps_g, lhsT, B(RB + dc * 512, 512),
                                        start=(dc == 0), stop=(dc == ND - 1))
                    if dc == ND - 1:
                        pl.inc("tensor", fg, "mm", 1)
                    else:
                        pl.emit("tensor", fg)
                pl.wait("scalar", "mm", pl.cnt["mm"])
                if pair_guard[pr]:
                    pl.wait("scalar", "vv", pair_guard[pr])
                sslot = RE + (t % 2) * 1536
                pl.inc("scalar", lambda e, ps_g=ps_g, t=t, sslot=sslot:
                       e.activation(B(sslot, 512), ps_g, AF.Silu, scale=small[:, 24 + t:25 + t]),
                       "ex", 1)
                pl.wait("vector", "ex", pl.cnt["ex"])
                uslot = RE + 512 + (t % 2) * 512
                if uq_guard[t % 2]:
                    pl.wait("vector", "uq", uq_guard[t % 2])
                def fu(e, ps_v=ps_v, t=t, uslot=uslot, sslot=sslot):
                    return e.scalar_tensor_tensor(B(uslot, 512), ps_v,
                                                  small[:, 24 + t:25 + t], B(sslot, 512),
                                                  ALU.mult, ALU.mult)
                pl.inc("vector", fu, "vv", 1)
                pair_guard[pr] = pl.cnt["vv"]
                pl.emit("vector", lambda e, uslot=uslot: e.reduce_max(small[:, 16:17], B(uslot, 512),
                                                                     axis=AX, apply_absolute_value=True))
                pl.emit("vector", lambda e: e.drain())
                pl.inc("vector", lambda e, t=t: e.tensor_tensor(small[:, 32 + t:33 + t],
                                                                small[:, 32 + t:33 + t],
                                                                small[:, 16:17], ALU.max), "vv", 1)
                pl.wait("sync", "vv", pl.cnt["vv"])
                uq_guard[t % 2] = pl.dma("sync", dt["u_scr"].ap()[t][:, og * 512:(og + 1) * 512],
                                         B(uslot, 512), "uq")
            panel_guard["RW2"] = pl.cnt["mm"]
        pl.wait("sync", "uq", pl.cnt["uq"])
        if dbg >= 2 and (dbgmask & 128):
            dbg_dump(small[:, 0:64])
        dbg_row[0] = 7

        phase_barrier(3)
        pl.wait("sync", "mm", pl.cnt["mm"])
        WOSLOT = [RA, RA + HID0, RB, RW]
        for quad in range(2):
            pl.wait("sync", "trv", pl.cnt.get("trv", 0))
            pl.wait("sync", "mm", pl.cnt["mm"])
            for tt4 in range(4):
                t = quad * 4 + tt4
                pl.emit("vector", lambda e, t=t: e.reciprocal(small[:, 16:17], small[:, 32 + t:33 + t]))
                pl.emit("vector", lambda e: e.drain())
                pl.emit("vector", lambda e: e.tensor_scalar(small[:, 5:6], small[:, 16:17],
                                                            127.0, None, ALU.mult))
                pl.emit("vector", lambda e, t=t: e.scalar_tensor_tensor(small[:, 40 + t:41 + t],
                                                                        small[:, 32 + t:33 + t], 1.0 / 127.0,
                                                                        small[:, 49:50], ALU.mult, ALU.mult))
                pl.emit("vector", lambda e: e.drain())
                slot = WOSLOT[tt4]
                ld_u = 0
                for c0 in range(0, HID0, D):
                    ld_u = pl.dma("sync", B(slot + c0, D), dt["u_scr"].ap()[t][:, c0:c0 + D], "pu")
                pl.wait("vector", "pu", ld_u)
                for c0 in range(0, HID0, D):
                    pl.emit("vector", lambda e, slot=slot, c0=c0:
                            e.tensor_scalar(stage[:, 0:D], B(slot + c0, D),
                                            small[:, 5:6], MAGIC, ALU.mult, ALU.add))
                    pl.emit("vector", lambda e: e.drain())
                    pl.emit("vector", lambda e, slot=slot, c0=c0:
                            e.tensor_scalar(B(slot + c0, D), stage[:, 0:D],
                                            MAGIC, None, ALU.subtract))
                    pl.emit("vector", lambda e: e.drain())
                pl.inc("vector", lambda e: e.engine_nop(), "vv", 1)
                pl.wait("tensor", "vv", pl.cnt["vv"])
                pairs = [(B(slot + hc * P, P), B(slot + hc * P, P)) for hc in range(64)]
                transpose_batch(pairs)
            uqT_done = pl.cnt["trv"]
            pl.wait("tensor", "trv", uqT_done)
            pl.wait("sync", "trv", uqT_done)
            WOPS = [ps_c[:, 0:512], ps_c[:, 512:1024], ps_d[:, 0:512], ps_a[:, 0:512]]
            for oc in range(4):
                for hg in range(8):
                    poff = RC + (hg % 2) * 4096
                    if panel_guard.get(("wo2", hg % 2)):
                        pl.wait("sync", "mm", panel_guard[("wo2", hg % 2)])
                    wl_c = 0
                    for i in range(8):
                        wl_c = pl.dma("sync", B(poff + i * 512, 512),
                                      dt["wo_qg"].ap()[hg * 8 + i][:, oc * 512:(oc + 1) * 512], "wl")
                    pl.wait("tensor", "wl", wl_c)
                    for tt4 in range(4):
                        for i in range(8):
                            lhsT = B(WOSLOT[tt4] + (hg * 8 + i) * P, P)
                            def fo(e, tt4=tt4, lhsT=lhsT, poff=poff, i=i, hg=hg):
                                return e.matmul(WOPS[tt4], lhsT, B(poff + i * 512, 512),
                                                start=(hg == 0 and i == 0),
                                                stop=(hg == 7 and i == 7))
                            if tt4 == 3 and i == 7:
                                pl.inc("tensor", fo, "mm", 1)
                            else:
                                pl.emit("tensor", fo)
                    panel_guard[("wo2", hg % 2)] = pl.cnt["mm"]
                pl.wait("vector", "mm", pl.cnt["mm"])
                for tt4 in range(4):
                    t = quad * 4 + tt4
                    dst = B(RQ + t * D + oc * 512, 512)
                    def fe2(e, tt4=tt4, t=t, dst=dst):
                        return e.scalar_tensor_tensor(dst, WOPS[tt4],
                                                      small[:, 40 + t:41 + t], dst,
                                                      ALU.mult, ALU.add)
                    pl.inc("vector", fe2, "vv", 1)
                pl.wait("tensor", "vv", pl.cnt["vv"])
        for t in range(NT):
            sandwich_tail(B(RQ + t * D, D), t)

    if n_sub >= 3:
        A2 = RB

        pl.wait("sync", "ex", pl.cnt["ex"])
        pl.wait("sync", "vv", pl.cnt["vv"])
        pl.wait("gpsimd", "vv", pl.cnt["vv"])
        for t in range(NT):
            xt = xa[:, t * D:(t + 1) * D]
            norm_stats(xt, D, LN_EPS, "ln")
            pl.inc("scalar", lambda e, xt=xt: e.activation(B(RE, D), xt, AF.Identity,
                                                           scale=small[:, 0:1], bias=small[:, 1:2]), "ex", 1)
            pl.wait("sync", "ex", pl.cnt["ex"])
            pl.dma("sync", dt["x_scr"].ap()[t], xt, "xs")
            pl.wait("vector", "ex", pl.cnt["ex"])
            pl.inc("vector", lambda e: e.engine_nop(), "vv", 1)
            norm_stats(B(RE, D), D, RMS1_EPS, "rms", rstd_col=2)
            pl.inc("scalar", lambda e: e.activation(B(RE, D), B(RE, D), AF.Identity,
                                                    scale=small[:, 2:3]), "ex", 1)
            pl.wait("tensor", "ex", pl.cnt["ex"])
            pairs = [(B(RE + dc * P, P), B(RA + dc * T_CORE + t * P, P)) for dc in range(ND)]
            transpose_batch(pairs)
        xnT_done = pl.cnt["trv"]
        xs_done = pl.cnt["xs"]

        def l1_proj(wname, dst):
            for oc in range(ND):
                half = oc % 2
                poff = RC + half * 2048
                if panel_guard.get(("l1", half)):
                    pl.wait("gpsimd", "mm", panel_guard[("l1", half)])
                wl_c = 0
                for dc2 in range(ND):
                    wl_c = pl.dma("gpsimd", B(poff + dc2 * P, P),
                                  dt[wname].ap()[dc2][:, oc * P:(oc + 1) * P], "wl")
                pl.wait("tensor", "wl", wl_c)
                for dc2 in range(ND):
                    for hf in range(2):
                        def f(e, poff=poff, dc2=dc2, hf=hf):
                            return e.matmul(ps_a[:, hf * 512:(hf + 1) * 512],
                                            B(poff + dc2 * P, P),
                                            B(RA + dc2 * T_CORE + hf * 512, 512),
                                            start=(dc2 == 0), stop=(dc2 == ND - 1))
                        if dc2 == ND - 1 and hf == 1:
                            pl.inc("tensor", f, "mm", 1)
                        else:
                            pl.emit("tensor", f)
                panel_guard[("l1", half)] = pl.cnt["mm"]
                pl.wait("scalar", "mm", pl.cnt["mm"])
                func = AF.Sigmoid if wname == "wgT1" else AF.Silu
                if dst is not None:
                    tgt = B(dst + oc * T_CORE, T_CORE)
                    pl.inc("scalar", lambda e, tgt=tgt, func=func:
                           e.activation(tgt, ps_a[:], func), "ex", 1)
                    pl.wait("tensor", "ex", pl.cnt["ex"])
                else:
                    if pl.cnt.get("gs", 0):
                        pl.wait("scalar", "gs", pl.cnt["gs"])
                    pl.inc("scalar", lambda e, func=func:
                           e.activation(B(RM, T_CORE), ps_a[:], func), "ex", 1)
                    pl.wait("sync", "ex", pl.cnt["ex"])
                    scr = "gt_scr" if wname == "wgT1" else "vt_scr"
                    pl.dma("sync", dt[scr].ap()[oc], B(RM, T_CORE), "gs")
                    pl.wait("tensor", "gs", pl.cnt["gs"])
                    pl.wait("tensor", "ex", pl.cnt["ex"])

        pl.wait("tensor", "trv", xnT_done)
        l1_proj("wqT1", RQ)
        l1_proj("wkT1", A2)
        for c in range(4):
            for tk in range(2):
                pst = ps_d[:, 0:256]
                for dc2 in range(ND):
                    def f(e, pst=pst, dc2=dc2, c=c, tk=tk):
                        return e.matmul(pst, B(A2 + dc2 * T_CORE + (2 * c + tk) * P, P),
                                        B(RQ + dc2 * T_CORE + c * 256, 256),
                                        start=(dc2 == 0), stop=(dc2 == ND - 1))
                    if dc2 == ND - 1:
                        pl.inc("tensor", f, "mm", 1)
                    else:
                        pl.emit("tensor", f)
                pl.wait("vector", "mm", pl.cnt["mm"])
                soff = RE + (c * 2 + tk) * 256
                if tk == 0:
                    pl.inc("vector", lambda e, soff=soff:
                           e.tensor_tensor(B(soff, 256), ps_d[:, 0:256], B(RM, 256), ALU.mult),
                           "vv", 1)
                else:
                    pl.emit("vector", lambda e, soff=soff: e.memset(B(soff, P), 0.0))
                    pl.emit("vector", lambda e: e.drain())
                    pl.inc("vector", lambda e, soff=soff:
                           e.tensor_tensor(B(soff + P, P), ps_d[:, 128:256], B(RM, P), ALU.mult),
                           "vv", 1)
                pl.wait("tensor", "vv", pl.cnt["vv"])
        l1_proj("wgT1", None)
        l1_proj("wvT1", None)

        pairs = []
        for t in range(NT):
            for dc in range(ND):
                pairs.append((B(A2 + dc * T_CORE + t * P, P), B(RA + (t * ND + dc) * P, P)))
        transpose_batch(pairs)
        pl.wait("sync", "gs", pl.cnt["gs"])
        pl.wait("sync", "mm", pl.cnt["mm"])
        pl.wait("sync", "trv", pl.cnt["trv"])
        vl_c = 0
        for ec in range(ND):
            vl_c = pl.dma("sync", B(RC + 4096 + (ec % 4) * T_CORE, T_CORE), dt["vt_scr"].ap()[ec], "kl")
            pl.wait("tensor", "kl", vl_c)
            pairs = [(B(RC + 4096 + (ec % 4) * T_CORE + t * P, P), B(A2 + (t * ND + ec) * P, P))
                     for t in range(NT)]
            transpose_batch(pairs)
            pl.wait("sync", "trv", pl.cnt["trv"])
        v_done = pl.cnt["trv"]

        pl.wait("vector", "xs", xs_done)
        pl.wait("tensor", "trv", v_done)
        oo_guard = [0, 0]
        for s in range(4):
            pl.wait("scalar", "ct", pl.cnt.get("ct", 0))
            for c in range(4):
                for et in range(4):
                    pst = ps_c[:, (et % 2) * 512:(et % 2) * 512 + 256]
                    first_done = False
                    if c > 0:
                        for dc2 in range(ND):
                            def f(e, pst=pst, dc2=dc2, et=et, c=c):
                                return e.matmul(pst, B(RC + dc2 * 512 + et * P, P),
                                                B(RQ + dc2 * T_CORE + c * 256, 256),
                                                start=(dc2 == 0), stop=False)
                            pl.emit("tensor", f)
                        first_done = True
                    for tk in range(2):
                        soff = RE + (c * 2 + tk) * 256
                        def f(e, pst=pst, s=s, c=c, tk=tk, et=et, soff=soff, first=(not first_done and tk == 0)):
                            return e.matmul(pst, B(A2 + ((2 * c + tk) * ND + s * 4 + et) * P, P),
                                            B(soff, 256), start=first, stop=(tk == 1))
                        if tk == 1:
                            pl.inc("tensor", f, "mm", 1)
                        else:
                            pl.emit("tensor", f)
                    pl.wait("vector", "mm", pl.cnt["mm"])
                    slot = RM + (et % 2) * 256
                    if oo_guard[et % 2]:
                        pl.wait("vector", "oo", oo_guard[et % 2])
                    pl.inc("vector", lambda e, pst=pst, slot=slot:
                           e.tensor_copy(B(slot, 256), pst), "vv", 1)
                    pl.wait("sync", "vv", pl.cnt["vv"])
                    oo_guard[et % 2] = pl.dma("sync", dt["oo_scr"].ap()[s * 4 + et][:, c * 256:(c + 1) * 256],
                                              B(slot, 256), "oo")
                    pl.wait("tensor", "vv", pl.cnt["vv"])
                for dc2 in range(ND):
                    pst = ps_d[:, 0:512] if (dc2 % 2 == 0) else ps_b[:, 0:512]
                    for tt in range(2):
                        def f(e, pst=pst, dc2=dc2, c=c, tt=tt, s=s):
                            return e.matmul(pst, B(RA + ((2 * c + tt) * ND + dc2) * P, P),
                                            B(A2 + ((2 * c + tt) * ND + s * 4) * P, 512),
                                            start=(tt == 0), stop=(tt == 1))
                        if tt == 1:
                            pl.inc("tensor", f, "mm", 1)
                        else:
                            pl.emit("tensor", f)
                    pl.wait("vector", "mm", pl.cnt["mm"])
                    sf = xa[:, dc2 * 512:(dc2 + 1) * 512]
                    if c == 0:
                        pl.inc("vector", lambda e, sf=sf, pst=pst: e.tensor_copy(sf, pst), "vv", 1)
                    else:
                        pl.inc("vector", lambda e, sf=sf, pst=pst:
                               e.tensor_tensor(sf, sf, pst, ALU.add), "vv", 1)
                    pl.wait("scalar", "vv", pl.cnt["vv"])
                    pl.inc("scalar", lambda e, sf=sf, dc2=dc2:
                           e.activation(B(RC + dc2 * 512, 512), sf, AF.Identity), "ex", 1)
                    pl.wait("tensor", "ex", pl.cnt["ex"])
                    pl.wait("tensor", "vv", pl.cnt["vv"])
            pl.wait("sync", "ex", pl.cnt["ex"])
            ct_c = 0
            for dc2 in range(ND):
                ct_c = pl.dma("sync", dt[f"ct_in{s}"].ap()[dc2], B(RC + dc2 * 512, 512), "ct")
            pl.wait("gpsimd", "ct", pl.cnt["ct"])
            pl.inc("gpsimd", lambda e, s=s: e.collective_compute(
                "AllGather", ALU.bypass, replica_groups=[list(range(8))],
                ins=[dt[f"ct_in{s}"].ap().opt()], outs=[dt[f"ct_g{s}"].ap().opt()]), "cc", 1)
        cc_ct = pl.cnt["cc"]

        pl.wait("sync", "cc", cc_ct)
        pm_c = pl.dma("sync", small[:, 56:64], dt["pmask"].ap(), "pm")
        pl.wait("vector", "pm", pm_c)
        pl.wait("sync", "mm", pl.cnt["mm"])
        pl.wait("sync", "ex", pl.cnt["ex"])
        pl.wait("sync", "vv", pl.cnt["vv"])
        for s in range(4):
            for j in range(8):
                cj = 0
                for dc2 in range(ND):
                    cj = pl.dma("sync", B(RA + dc2 * 512, 512), dt[f"ct_g{s}"].ap()[j * ND + dc2], "pj")
                pl.wait("vector", "pj", cj)
                if j == 0:
                    pl.inc("vector", lambda e: e.tensor_scalar(xa[:, 0:8192], B(RA, 8192),
                                                               small[:, 56:57], None, ALU.mult), "vv", 1)
                else:
                    pl.inc("vector", lambda e, j=j: e.scalar_tensor_tensor(xa[:, 0:8192], B(RA, 8192),
                                                                           small[:, 56 + j:57 + j],
                                                                           xa[:, 0:8192], ALU.mult, ALU.add), "vv", 1)
                pl.wait("sync", "vv", pl.cnt["vv"])
            pl.wait("scalar", "vv", pl.cnt["vv"])
            pl.inc("scalar", lambda e: e.activation(B(RC, 8192), xa[:, 0:8192], AF.Identity), "ex", 1)
            pl.wait("tensor", "ex", pl.cnt["ex"])
            for et in range(4):
                for dc2 in range(ND):
                    for hf in range(2):
                        def f(e, dc2=dc2, et=et, hf=hf):
                            return e.matmul(ps_a[:, hf * 512:(hf + 1) * 512],
                                            B(RC + dc2 * 512 + et * P, P),
                                            B(RQ + dc2 * T_CORE + hf * 512, 512),
                                            start=(dc2 == 0), stop=(dc2 == ND - 1))
                        if dc2 == ND - 1 and hf == 1:
                            pl.inc("tensor", f, "mm", 1)
                        else:
                            pl.emit("tensor", f)
                ec = s * 4 + et
                ol_c = pl.dma("sync", B(RA + 8192, T_CORE), dt["oo_scr"].ap()[ec], "pj")
                ol_c = pl.dma("sync", B(RA + 8192 + T_CORE, T_CORE), dt["gt_scr"].ap()[ec], "pj")
                pl.wait("vector", "pj", ol_c)
                pl.wait("vector", "mm", pl.cnt["mm"])
                pl.emit("vector", lambda e: e.tensor_tensor(den[:, 0:T_CORE], ps_a[:],
                                                            B(RA + 8192, T_CORE), ALU.add))
                pl.emit("vector", lambda e: e.drain())
                pl.inc("vector", lambda e, ec=ec:
                       e.tensor_tensor(B(A2 + ec * T_CORE, T_CORE), den[:, 0:T_CORE],
                                       B(RA + 8192 + T_CORE, T_CORE), ALU.mult), "vv", 1)
                pl.wait("tensor", "vv", pl.cnt["vv"])
                pl.wait("sync", "vv", pl.cnt["vv"])
        og_done = pl.cnt["vv"]

        pl.wait("gpsimd", "vv", og_done)
        pl.wait("gpsimd", "mm", pl.cnt["mm"])
        for og4 in range(4):
            wl_c = 0
            for ec in range(ND):
                wl_c = pl.dma("gpsimd", B(RC + ec * 512, 512),
                              dt["wpT1"].ap()[ec][:, og4 * 512:(og4 + 1) * 512], "wl")
            pl.wait("tensor", "wl", wl_c)
            for t in range(NT):
                pst = ps_d[:, 0:512] if (t % 2) else ps_c[:, 0:512]
                for ec in range(ND):
                    def f(e, pst=pst, ec=ec, t=t):
                        return e.matmul(pst, B(A2 + ec * T_CORE + t * P, P),
                                        B(RC + ec * 512, 512),
                                        start=(ec == 0), stop=(ec == ND - 1))
                    if ec == ND - 1:
                        pl.inc("tensor", f, "mm", 1)
                    else:
                        pl.emit("tensor", f)
                pl.wait("vector", "mm", pl.cnt["mm"])
                dst = B(RQ + t * D + og4 * 512, 512)
                pl.inc("vector", lambda e, pst=pst, dst=dst: e.tensor_copy(dst, pst), "vv", 1)
                pl.wait("tensor", "vv", pl.cnt["vv"])
            pl.wait("gpsimd", "mm", pl.cnt["mm"])
        pl.wait("sync", "mm", pl.cnt["mm"])
        pl.wait("sync", "vv", pl.cnt["vv"])
        xr_c = 0
        for t in range(NT):
            xr_c = pl.dma("sync", xa[:, t * D:(t + 1) * D], dt["x_scr"].ap()[t], "xr")
        pl.wait("vector", "xr", xr_c)
        pl.wait("scalar", "xr", xr_c)
        for t in range(NT):
            sandwich_tail(B(RQ + t * D, D), t)

    if n_sub >= 4:
        pl.wait("sync", "ex", pl.cnt["ex"])
        pl.wait("sync", "vv", pl.cnt["vv"])
        pl.wait("sync", "mm", pl.cnt["mm"])
        for srcn, nr, wdt, qdst, icol in (("w1_sl", 2, HID1, "w1_q", 52),
                                          ("w2_sl", 2, HID1, "w2_q", 53),
                                          ("w3_sl", 44, 256, "w3_q", 54)):
            for r in range(nr):
                for c0 in range(0, wdt, D):
                    cw = min(D, wdt - c0)
                    ld_c = pl.dma("sync", stage[:, 0:cw], dt[srcn].ap()[r][:, c0:c0 + cw], "pp")
                    pl.wait("vector", "pp", ld_c)
                    pl.emit("vector", lambda e, icol=icol, cw=cw:
                            e.tensor_scalar(stage[:, 0:cw], stage[:, 0:cw],
                                            small[:, icol:icol + 1], MAGIC, ALU.mult, ALU.add))
                    pl.emit("vector", lambda e: e.drain())
                    pl.emit("vector", lambda e, cw=cw: e.tensor_scalar(B(RB, cw), stage[:, 0:cw],
                                                                       MAGIC, None, ALU.subtract))
                    pl.emit("vector", lambda e: e.drain())
                    pl.inc("vector", lambda e, cw=cw: e.tensor_scalar(B(RB, cw), B(RB, cw), 1.0, -1.0,
                                                                      ALU.min, ALU.max), "vv", 1)
                    pl.wait("sync", "vv", pl.cnt["vv"])
                    qd_c = pl.dma("sync", dt[qdst].ap()[r][:, c0:c0 + cw], B(RB, cw), "pq")
                    pl.wait("vector", "pq", qd_c)
        pl.wait("gpsimd", "pq", pl.cnt["pq"])
        for w in ("w1", "w2", "w3"):
            pl.inc("gpsimd", lambda e, w=w: e.collective_compute(
                "AllGather", ALU.bypass, replica_groups=[list(range(8))],
                ins=[dt[f"{w}_q"].ap().opt()], outs=[dt[f"{w}_qg"].ap().opt()]), "cc", 1)
        cc_w = pl.cnt["cc"]

        for t in range(NT):
            xt = xa[:, t * D:(t + 1) * D]
            norm_stats(xt, D, LN_EPS, "ln")
            pl.inc("scalar", lambda e, xt=xt: e.activation(B(RE, D), xt, AF.Identity,
                                                           scale=small[:, 0:1], bias=small[:, 1:2]), "ex", 1)
            pl.wait("tensor", "ex", pl.cnt["ex"])
            pairs = [(B(RE + dc * P, P), B(RA + (t * ND + dc) * P, P)) for dc in range(ND)]
            transpose_batch(pairs)
        xlT4_done = pl.cnt["trv"]

        pl.wait("sync", "cc", cc_w)
        pl.wait("sync", "trv", xlT4_done)
        pl.wait("tensor", "trv", xlT4_done)
        pair_guard4 = [0, 0]
        hq_guard = [0, 0]
        for hp in range(11):
            if panel_guard.get("l1f"):
                pl.wait("sync", "mm", panel_guard["l1f"])
            wl_c = 0
            for dc in range(ND):
                pl.dma("sync", B(RW + dc * 512, 512), dt["w1_qg"].ap()[dc][:, hp * 512:(hp + 1) * 512], "wl")
                wl_c = pl.dma("sync", B(RB + dc * 512, 512), dt["w2_qg"].ap()[dc][:, hp * 512:(hp + 1) * 512], "wl")
            pl.wait("tensor", "wl", wl_c)
            for t in range(NT):
                pr = t % 2
                ps_1 = ps_d[:, 0:512] if pr == 0 else ps_a[:, 0:512]
                ps_2 = ps_c[:, 0:512] if pr == 0 else ps_b[:, 0:512]
                if pair_guard4[pr]:
                    pl.wait("tensor", "vv", pair_guard4[pr])
                for dc in range(ND):
                    lhsT = B(RA + (t * ND + dc) * P, P)
                    def f1(e, ps_1=ps_1, lhsT=lhsT, dc=dc):
                        return e.matmul(ps_1, lhsT, B(RW + dc * 512, 512),
                                        start=(dc == 0), stop=(dc == ND - 1))
                    pl.emit("tensor", f1)
                for dc in range(ND):
                    lhsT = B(RA + (t * ND + dc) * P, P)
                    def f2(e, ps_2=ps_2, lhsT=lhsT, dc=dc):
                        return e.matmul(ps_2, lhsT, B(RB + dc * 512, 512),
                                        start=(dc == 0), stop=(dc == ND - 1))
                    if dc == ND - 1:
                        pl.inc("tensor", f2, "mm", 1)
                    else:
                        pl.emit("tensor", f2)
                pl.wait("scalar", "mm", pl.cnt["mm"])
                if pair_guard4[pr]:
                    pl.wait("scalar", "vv", pair_guard4[pr])
                sslot = RE + (t % 2) * 1024
                pl.inc("scalar", lambda e, ps_1=ps_1, sslot=sslot:
                       e.activation(B(sslot, 512), ps_1, AF.Sigmoid), "ex", 1)
                pl.wait("vector", "ex", pl.cnt["ex"])
                if hq_guard[t % 2]:
                    pl.wait("vector", "hq", hq_guard[t % 2])
                def fh(e, ps_2=ps_2, sslot=sslot):
                    return e.tensor_tensor(B(sslot + 512, 512), B(sslot, 512), ps_2, ALU.mult)
                pl.inc("vector", fh, "vv", 1)
                pair_guard4[pr] = pl.cnt["vv"]
                pl.wait("sync", "vv", pl.cnt["vv"])
                hq_guard[t % 2] = pl.dma("sync", dt["hid_scr"].ap()[t][:, hp * 512:(hp + 1) * 512],
                                         B(sslot + 512, 512), "hq")
            panel_guard["l1f"] = pl.cnt["mm"]
        pl.wait("sync", "hq", pl.cnt["hq"])

        NH = 44
        W3SLOT = [RA, RA + 5632, RA + 11264, RA + 16896]
        GRP = [8, 8, 8, 8, 8, 4]
        GBASE = [0, 8, 16, 24, 32, 40]
        for quad in range(2):
            pl.wait("sync", "trv", pl.cnt.get("trv", 0))
            pl.wait("sync", "mm", pl.cnt["mm"])
            for tt4 in range(4):
                t = quad * 4 + tt4
                slot = W3SLOT[tt4]
                ld_u = 0
                for c0 in range(0, HID1, D):
                    cw = min(D, HID1 - c0)
                    ld_u = pl.dma("sync", B(slot + c0, cw), dt["hid_scr"].ap()[t][:, c0:c0 + cw], "pu")
                pl.wait("tensor", "pu", ld_u)
                pairs = [(B(slot + hc * P, P), B(slot + hc * P, P)) for hc in range(NH)]
                transpose_batch(pairs)
            uqT_done = pl.cnt["trv"]
            pl.wait("tensor", "trv", uqT_done)
            pl.wait("sync", "trv", uqT_done)
            WOPS = [ps_c[:, 0:512], ps_c[:, 512:1024], ps_d[:, 0:512], ps_a[:, 0:512]]
            for oc in range(4):
                for hg in range(6):
                    poff = RC + (hg % 2) * 4096
                    if panel_guard.get(("w32", hg % 2)):
                        pl.wait("sync", "mm", panel_guard[("w32", hg % 2)])
                    wl_c = 0
                    for i in range(GRP[hg]):
                        for jj in range(2):
                            wl_c = pl.dma("sync", B(poff + i * 512 + jj * 256, 256),
                                          dt["w3_qg"].ap()[(2 * oc + jj) * NH + GBASE[hg] + i][:, 0:256], "wl")
                    pl.wait("tensor", "wl", wl_c)
                    for tt4 in range(4):
                        for i in range(GRP[hg]):
                            lhsT = B(W3SLOT[tt4] + (GBASE[hg] + i) * P, P)
                            def fo(e, tt4=tt4, lhsT=lhsT, poff=poff, i=i, hg=hg):
                                return e.matmul(WOPS[tt4], lhsT, B(poff + i * 512, 512),
                                                start=(hg == 0 and i == 0),
                                                stop=(hg == 5 and i == GRP[5] - 1))
                            if tt4 == 3 and i == GRP[hg] - 1:
                                pl.inc("tensor", fo, "mm", 1)
                            else:
                                pl.emit("tensor", fo)
                    panel_guard[("w32", hg % 2)] = pl.cnt["mm"]
                pl.wait("vector", "mm", pl.cnt["mm"])
                for tt4 in range(4):
                    t = quad * 4 + tt4
                    dst = B(RQ + t * D + oc * 512, 512)
                    def fe2(e, tt4=tt4, dst=dst):
                        return e.tensor_copy(dst, WOPS[tt4])
                    pl.inc("vector", fe2, "vv", 1)
                pl.wait("tensor", "vv", pl.cnt["vv"])
        for t in range(NT):
            sandwich_tail(B(RQ + t * D, D), t)

    pl.wait("sync", "vv", pl.cnt["vv"])
    pl.wait("sync", "ex", pl.cnt["ex"])
    for t in range(NT):
        pl.dma("sync", dt["out8"].ap()[t], xa[:, t * D:(t + 1) * D], "st")
    pl.wait("sync", "st", pl.cnt["st"])
    if dbg and n_sub == 3:
        pl.wait("gpsimd", "st", pl.cnt["st"])
        pl.wait("gpsimd", "vv", pl.cnt["vv"])
        pl.wait("gpsimd", "ex", pl.cnt["ex"])
        for ec in range(8):
            pl.dma("gpsimd", dt["dbg"].ap()[16 + ec][:, 0:T_CORE], B(RB + ec * T_CORE, T_CORE), "st")
        for ec in range(4):
            ld_d = pl.dma("gpsimd", stage[:, 0:T_CORE], dt["oo_scr"].ap()[ec], "pp")
            pl.wait("gpsimd", "pp", ld_d)
            pl.dma("gpsimd", dt["dbg"].ap()[24 + ec][:, 0:T_CORE], stage[:, 0:T_CORE], "st")
            pl.wait("gpsimd", "st", pl.cnt["st"])
        pl.dma("gpsimd", dt["dbg"].ap()[28][:, 0:2048], B(RE, 2048), "st")
        pl.dma("gpsimd", dt["dbg"].ap()[41][:, 0:64], small[:, 0:64], "st")
        for dc2 in range(2):
            pl.dma("gpsimd", dt["dbg"].ap()[33 + dc2][:, 0:512], B(RC + dc2 * 512, 512), "st")
        for j in range(3):
            for dc2 in range(2):
                ld_d = pl.dma("gpsimd", stage[:, 0:512], dt["ct_g3"].ap()[j * ND + dc2], "pp")
                pl.wait("gpsimd", "pp", ld_d)
                pl.dma("gpsimd", dt["dbg"].ap()[35 + j * 2 + dc2][:, 0:512], stage[:, 0:512], "st")
                pl.wait("gpsimd", "st", pl.cnt["st"])
        for ec in range(4):
            ld_d = pl.dma("gpsimd", stage[:, 0:T_CORE], dt["gt_scr"].ap()[ec], "pp")
            pl.wait("gpsimd", "pp", ld_d)
            pl.dma("gpsimd", dt["dbg"].ap()[29 + ec][:, 0:T_CORE], stage[:, 0:T_CORE], "st")
            pl.wait("gpsimd", "st", pl.cnt["st"])
        pl.wait("gpsimd", "st", pl.cnt["st"])
    if dbg and n_sub == 2:
        pl.wait("gpsimd", "st", pl.cnt["st"])
        pl.wait("gpsimd", "vv", pl.cnt["vv"])
        pl.wait("gpsimd", "ex", pl.cnt["ex"])
        ld_d = pl.dma("gpsimd", stage[:, 0:D], dt["u_scr"].ap()[0][:, 0:D], "pp")
        pl.wait("gpsimd", "pp", ld_d)
        pl.dma("gpsimd", dt["dbg"].ap()[11], stage[:, 0:D], "st")
        pl.wait("gpsimd", "st", pl.cnt["st"])
        ld_d = pl.dma("gpsimd", stage[:, 0:D], dt["gv_q"].ap()[0][:, 0:D], "pp")
        pl.wait("gpsimd", "pp", ld_d)
        pl.dma("gpsimd", dt["dbg"].ap()[12], stage[:, 0:D], "st")
        pl.wait("gpsimd", "st", pl.cnt["st"])
        pl.dma("gpsimd", dt["dbg"].ap()[13][:, 0:64], small[:, 0:64], "st")
        pl.wait("gpsimd", "st", pl.cnt["st"])
        for tt in range(NT):
            pl.dma("gpsimd", dt["dbg"].ap()[16 + tt], B(RQ + tt * D, D), "st")
        pl.wait("gpsimd", "st", pl.cnt["st"])
        for tt in range(6):
            for cq in range(4):
                ld_d = pl.dma("gpsimd", stage[:, 0:D], dt["u_scr"].ap()[tt][:, cq * D:(cq + 1) * D], "pp")
                pl.wait("gpsimd", "pp", ld_d)
                pl.dma("gpsimd", dt["dbg"].ap()[24 + tt * 4 + cq], stage[:, 0:D], "st")
                pl.wait("gpsimd", "st", pl.cnt["st"])
        for cq in range(4):
            ld_d = pl.dma("gpsimd", stage[:, 0:D], dt["u_scr"].ap()[0][:, cq * D:(cq + 1) * D], "pp")
            pl.wait("gpsimd", "pp", ld_d)
            pl.dma("gpsimd", dt["dbg"].ap()[2 if cq == 0 else 2][:, 0:D] if False else dt["dbg"].ap()[{0:1,1:2,2:3,3:8}[cq]], stage[:, 0:D], "st")
            pl.wait("gpsimd", "st", pl.cnt["st"])
        ld_d = pl.dma("gpsimd", stage[:, 0:D], dt["wo_q"].ap()[0][:, 0:D], "pp")
        pl.wait("gpsimd", "pp", ld_d)
        pl.dma("gpsimd", dt["dbg"].ap()[0], stage[:, 0:D], "st")
        pl.wait("gpsimd", "st", pl.cnt["st"])

    pl.build()
    pl.stack.close()
    return nc


def _mk_inputs_core(i, full, n_sub):
    f32 = np.float32
    x = np.ascontiguousarray(full["L0_attn_x"][0])
    xs = np.ascontiguousarray(x[i * T_CORE:(i + 1) * T_CORE]).reshape(NT, P, D)
    pos = np.arange(i * T_CORE, (i + 1) * T_CORE, dtype=np.float64)
    theta = 1.0 / 10000.0 ** (np.arange(0, HD, 2, dtype=f32) / HD)
    freqs = (pos[:, None] * theta[None, :].astype(np.float64)).astype(f32)
    cos = np.ascontiguousarray(np.cos(freqs)).reshape(NT, P, HD // 2)
    sin = np.ascontiguousarray(np.sin(freqs)).reshape(NT, P, HD // 2)
    shift = np.zeros((P, NKT), f32)
    shift[:, 8 * i:] = -KILL
    k_idx = np.arange(P)[:, None]
    q_idx = np.arange(T_CORE)[None, :]
    dmask0 = (k_idx <= q_idx).astype(f32)
    bias4 = np.zeros((4, D), f32)
    bias4[0] = full["L0_attn_bq"]
    bias4[1, :HKV * HD] = full["L0_attn_bk"]
    bias4[2, :HKV * HD] = full["L0_attn_bv"]
    bias4[3] = full["L0_attn_bo"]
    m = {
        "x8": xs, "cos8": cos, "sin8": sin,
        "Wq": np.ascontiguousarray(full["L0_attn_Wq"]).reshape(ND, P, D),
        "Wk": np.ascontiguousarray(full["L0_attn_Wk"]).reshape(ND, P, HKV * HD),
        "Wv": np.ascontiguousarray(full["L0_attn_Wv"]).reshape(ND, P, HKV * HD),
        "Wo": np.ascontiguousarray(full["L0_attn_Wo"]).reshape(ND, P, D),
        "bias4": bias4, "shift": shift, "dmask0": dmask0,
        "ident": np.eye(P, dtype=f32),
    }
    if n_sub >= 2:
        m["w_norm"] = full["L0_ffn_w_norm"].reshape(1, D)
        gvT = np.ascontiguousarray(full["L0_ffn_w_gv"].T)
        m["gv_sl"] = np.ascontiguousarray(gvT[i * 256:(i + 1) * 256]).reshape(2, P, 2 * HID0)
        woT = np.ascontiguousarray(full["L0_ffn_w_out"].T)
        m["wo_sl"] = np.ascontiguousarray(woT[i * 1024:(i + 1) * 1024]).reshape(NT, P, D)
    if n_sub >= 3:
        for nm, key in (("wqT1", "L1_attn_Wq"), ("wkT1", "L1_attn_Wk"),
                        ("wvT1", "L1_attn_Wv"), ("wgT1", "L1_attn_Wg"),
                        ("wpT1", "L1_attn_Wp")):
            m[nm] = np.ascontiguousarray(full[key].T).reshape(ND, P, D)
        pm = np.zeros((P, 8), f32)
        pm[:, :i] = 1.0
        m["pmask"] = pm
    if n_sub >= 4:
        w1T = np.ascontiguousarray(full["L1_ffn_w1"].T)
        m["w1_sl"] = np.ascontiguousarray(w1T[i * 256:(i + 1) * 256]).reshape(2, P, HID1)
        w2T = np.ascontiguousarray(full["L1_ffn_w2"].T)
        m["w2_sl"] = np.ascontiguousarray(w2T[i * 256:(i + 1) * 256]).reshape(2, P, HID1)
        w3T = np.ascontiguousarray(full["L1_ffn_w3"].T)
        m["w3_sl"] = np.ascontiguousarray(w3T[:, i * 256:(i + 1) * 256]).reshape(44, P, 256)
    return m


_NC_CACHE = {}
N_SUB = 4
BARRIERS = 0


def kernel(**inputs):
    n_sub = N_SUB
    key = (n_sub, BARRIERS)
    if key not in _NC_CACHE:
        _NC_CACHE[key] = build_program(n_sub, barriers=BARRIERS)
    nc = _NC_CACHE[key]
    full = {k: np.asarray(v, np.float32) for k, v in inputs.items()}
    in_maps = [_mk_inputs_core(i, full, n_sub) for i in range(8)]
    res = run_bass_kernel_spmd(nc, in_maps, core_ids=list(range(8)))
    out = np.concatenate([np.asarray(res.results[i]["out8"], np.float32).reshape(T_CORE, D)
                          for i in range(8)], axis=0)
    return out.reshape(1, SEQ, D)
